# version 01
# baseline (speedup 1.0000x reference)
import numpy as np
import ml_dtypes
import concourse.bass as bass
import concourse.mybir as mybir
from concourse.bass_utils import run_bass_kernel_spmd

F32 = mybir.dt.float32
BF16 = mybir.dt.bfloat16
AF = mybir.ActivationFunctionType
ALU = mybir.AluOpType
AX = mybir.AxisListType.X
PI = float(np.pi)
LN_EPS = 1e-5
ATTN_EPS = 1e-6

TOK = 2048
NT128 = TOK // 128
TT = 512
NTT = TOK // TT
DK = 8
DFF = 32
NHD = 8
NSL = 8

ENGS = ("sync", "gpsimd", "vector", "scalar", "tensor")


class Prog:
    def __init__(self):
        self.q = {e: [] for e in ENGS}
        self.last_wait = {e: {} for e in ENGS}

    def op(self, eng, fn, waits=(), incs=()):
        ws = []
        lw = self.last_wait[eng]
        for w in waits:
            if w is None:
                continue
            sem, val = w
            key = id(sem)
            if lw.get(key, 0) >= val:
                continue
            lw[key] = val
            ws.append((sem, val))

        def run(e, fn=fn, ws=tuple(ws), incs=tuple(incs)):
            for sem, val in ws:
                e.wait_ge(sem, val)
            ins = fn(e)
            for sem, n in incs:
                ins = ins.then_inc(sem, n)

        self.q[eng].append(run)


class Ctr:
    def __init__(self, h):
        self.h = h
        self.n = 0

    def bump(self, k=1):
        self.n += k
        return (self.h, k)

    def evt(self):
        return (self.h, self.n)


def build_nc(stage="full"):
    nc = bass.Bass("TRN2", target_bir_lowering=False)

    xo = nc.dram_tensor("x_own", [TOK, 1024], F32, kind="ExternalInput").ap()
    xt = nc.dram_tensor("x_oth", [TOK, 1024], F32, kind="ExternalInput").ap()
    wq_d = nc.dram_tensor("wq", [1024, 1024], F32, kind="ExternalInput").ap()
    wk_d = nc.dram_tensor("wk", [1024, 1024], F32, kind="ExternalInput").ap()
    wv_d = nc.dram_tensor("wv", [1024, 1024], F32, kind="ExternalInput").ap()
    wo_d = nc.dram_tensor("wo", [1024, 1024], F32, kind="ExternalInput").ap()
    w1_d = nc.dram_tensor("w1", [1024, 4096], F32, kind="ExternalInput").ap()
    w2_d = nc.dram_tensor("w2", [4096, 1024], F32, kind="ExternalInput").ap()
    bq_d = nc.dram_tensor("bq", [128, 8], F32, kind="ExternalInput").ap()
    bk_d = nc.dram_tensor("bk", [128, 8], F32, kind="ExternalInput").ap()
    bv_d = nc.dram_tensor("bv", [128, 8], F32, kind="ExternalInput").ap()
    bo_d = nc.dram_tensor("bo", [128, 8], F32, kind="ExternalInput").ap()
    b2_d = nc.dram_tensor("b2", [128, 8], F32, kind="ExternalInput").ap()
    b1_d = nc.dram_tensor("b1", [128, 32], F32, kind="ExternalInput").ap()
    th_d = nc.dram_tensor("theta", [128, 32], F32, kind="ExternalInput").ap()
    wr_d = nc.dram_tensor("wreal", [128, 32], F32, kind="ExternalInput").ap()
    wi_d = nc.dram_tensor("wimag", [128, 32], F32, kind="ExternalInput").ap()
    idb_d = nc.dram_tensor("idb", [128, 128], BF16, kind="ExternalInput").ap()
    idf_d = nc.dram_tensor("idf", [128, 128], F32, kind="ExternalInput").ap()
    y_d = nc.dram_tensor("y", [TOK, 1024], F32, kind="ExternalOutput").ap()

    dbg = {}
    if stage == "ln":
        dbg["xnT"] = nc.dram_tensor("dbg_xnT", [128, DK * TOK], BF16, kind="ExternalOutput").ap()
        dbg["cst"] = nc.dram_tensor("dbg_cst", [128, 6 * 32], F32, kind="ExternalOutput").ap()
    if stage == "a1":
        dbg["ks"] = nc.dram_tensor("dbg_ks", [128, 8], F32, kind="ExternalOutput").ap()
        dbg["kv"] = nc.dram_tensor("dbg_kv", [128, 8], F32, kind="ExternalOutput").ap()
    if stage == "fulldbg":
        dbg["act"] = nc.dram_tensor("dbg_act", [128, DFF * TT], BF16, kind="ExternalOutput").ap()

    A = nc.alloc_sbuf_tensor
    xnT_h = A("xnT", [128, DK * TOK], BF16)
    act_h = A("act", [128, DFF * TT], BF16)
    wsm_h = A("wsm", [128, 2 * 8192], BF16)
    w1r_h = A("w1r", [128, 2 * DK * 512], BF16)
    w2r_h = A("w2r", [128, 2 * 4 * 512], BF16)
    gfm_h = A("gfm", [128, 8 * 512], F32)
    attn_h = A("attn", [128, 8 * 512], BF16)
    xls_h = A("xls", [128, 2 * 1024], F32)
    xrs_h = A("xrs", [128, 4 * 1024], F32)
    ysb_h = A("ysb", [128, 4 * 1024], F32)
    pool_h = A("pool", [128, 8 * 512], F32)
    idb_h = A("idbs", [128, 128], BF16)
    idf_h = A("idfs", [128, 128], F32)
    cst_h = A("cst", [128, 16 * 32], F32)
    bia_h = A("bia", [128, 8 * 8], F32)
    bn_h = A("bnsc", [128, 96], F32)
    kkv_h = A("kkv", [128, 2 * 8], F32)
    kac_h = A("kac", [128, 2 * 64], F32)

    xnT = xnT_h.ap().rearrange("p (a b) -> p a b", a=DK)
    xnO = act_h.ap().rearrange("p (a b) -> p a b", a=DK)
    actv = act_h.ap().rearrange("p (a b) -> p a b", a=DFF)
    wsm = wsm_h.ap().rearrange("p (w a b c) -> p w a b c", w=2, a=DK, b=8)
    w1r = w1r_h.ap().rearrange("p (s a b) -> p s a b", s=2, a=DK)
    w2r = w2r_h.ap().rearrange("p (s a b) -> p s a b", s=2, a=4)
    gfm = gfm_h.ap().rearrange("p (a b) -> p a b", a=8)
    attn = attn_h.ap().rearrange("p (a b) -> p a b", a=NHD)
    xls = xls_h.ap().rearrange("p (a b) -> p a b", a=2)
    xrs = xrs_h.ap().rearrange("p (a b) -> p a b", a=4)
    ysb = ysb_h.ap().rearrange("p (a b) -> p a b", a=4)
    pool = pool_h.ap().rearrange("p (a b) -> p a b", a=8)
    idb = idb_h.ap()
    idf = idf_h.ap()
    cst = cst_h.ap().rearrange("p (a b) -> p a b", a=16)
    bia = bia_h.ap().rearrange("p (a b) -> p a b", a=8)
    bns = bn_h.ap()
    kkv = kkv_h.ap().rearrange("p (a b) -> p a b", a=2)
    kac = kac_h.ap().rearrange("p (a b c) -> p a b c", a=2, b=8)

    C_TH, C_THW, C_STH, C_CTH, C_WR, C_WI, C_C, C_T1, C_T2, C_T3, C_TH1, C_STH1, C_CTH1, C_B1, C_K = range(15)
    B_Q, B_K, B_V, B_O, B_2 = range(5)
    LM, LV, LR, LMR, LJ = 0, 16, 32, 48, 64

    rem = nc.sbuf_bytes_remaining
    assert rem >= 0, f"SBUF over budget by {-rem} bytes"

    pb = [nc.alloc_psum_tensor(f"pb{i}", [128, 512], F32).ap() for i in range(8)]

    import contextlib
    stack = contextlib.ExitStack()
    sx = Ctr(stack.enter_context(nc.semaphore("sx")))
    sw = Ctr(stack.enter_context(nc.semaphore("sw")))
    sT = Ctr(stack.enter_context(nc.semaphore("sT")))
    sS = Ctr(stack.enter_context(nc.semaphore("sS")))
    sV = Ctr(stack.enter_context(nc.semaphore("sV")))
    sy = Ctr(stack.enter_context(nc.semaphore("sy")))

    P = Prog()
    bank_free = [None] * 8
    bank_data = [None] * 8
    st = {}

    def dma(eng, ctr, dst, src, waits=()):
        inc = ctr.bump(16)
        P.op(eng, lambda e, d=dst, s=src: e.dma_start(out=d, in_=s), waits=waits, incs=[inc])
        return ctr.evt()

    def vop(fn, waits=(), bump=False, drain=False):
        incs = [sV.bump()] if bump else []
        P.op("vector", fn, waits=waits, incs=incs)
        if drain:
            P.op("vector", lambda e: e.drain())
        return sV.evt() if bump else None

    def sop(fn, waits=(), bump=False):
        incs = [sS.bump()] if bump else []
        P.op("scalar", fn, waits=waits, incs=incs)
        return sS.evt() if bump else None

    def top(fn, waits=(), bump=False):
        incs = [sT.bump()] if bump else []
        P.op("tensor", fn, waits=waits, incs=incs)
        return sT.evt() if bump else None

    ev_idb = dma("sync", sx, idb[:, :], idb_d)
    ev_idf = dma("sync", sx, idf[:, :], idf_d)
    for col, src in [(B_Q, bq_d), (B_K, bk_d), (B_V, bv_d), (B_O, bo_d), (B_2, b2_d)]:
        dma("sync", sx, bia[:, col, :], src)
    for col, src in [(C_TH, th_d), (C_WR, wr_d), (C_WI, wi_d), (C_B1, b1_d)]:
        dma("sync", sx, cst[:, col, :], src)
    ev_cst_loads = sx.evt()

    def wrap_once(buf_c, tmp_c, lower=True, bump_last=False):
        if lower:
            vop(lambda e, b=buf_c, t=tmp_c: e.tensor_scalar(
                out=cst[:, t, :], in0=cst[:, b, :], scalar1=-PI, scalar2=2 * PI,
                op0=ALU.is_lt, op1=ALU.mult), drain=True)
            vop(lambda e, b=buf_c, t=tmp_c: e.tensor_tensor(
                out=cst[:, b, :], in0=cst[:, b, :], in1=cst[:, t, :], op=ALU.add), drain=True)
        vop(lambda e, b=buf_c, t=tmp_c: e.tensor_scalar(
            out=cst[:, t, :], in0=cst[:, b, :], scalar1=PI, scalar2=2 * PI,
            op0=ALU.is_gt, op1=ALU.mult), drain=True)
        return vop(lambda e, b=buf_c, t=tmp_c: e.tensor_tensor(
            out=cst[:, b, :], in0=cst[:, b, :], in1=cst[:, t, :], op=ALU.subtract),
            bump=bump_last, drain=True)

    vop(lambda e: e.tensor_copy(out=cst[:, C_THW, :], in_=cst[:, C_TH, :]),
        waits=[ev_cst_loads], drain=True)
    for i in range(7):
        ev_thw = wrap_once(C_THW, C_T1, bump_last=(i == 6))
    vop(lambda e: e.tensor_scalar(out=cst[:, C_T2, :], in0=cst[:, C_THW, :],
                                  scalar1=PI / 2, scalar2=None, op0=ALU.add, op1=ALU.bypass),
        drain=True)
    ev_cosarg = wrap_once(C_T2, C_T1, lower=False, bump_last=True)
    ev_sth = sop(lambda e: e.activation(out=cst[:, C_STH, :], in_=cst[:, C_THW, :], func=AF.Sin),
                 waits=[ev_thw], bump=True)
    ev_cth = sop(lambda e: e.activation(out=cst[:, C_CTH, :], in_=cst[:, C_T2, :], func=AF.Sin),
                 waits=[ev_cosarg], bump=True)
    vop(lambda e: e.tensor_tensor(out=cst[:, C_T1, :], in0=cst[:, C_CTH, :], in1=cst[:, C_WR, :], op=ALU.mult),
        waits=[ev_cth, ev_sth], drain=True)
    vop(lambda e: e.tensor_tensor(out=cst[:, C_T2, :], in0=cst[:, C_STH, :], in1=cst[:, C_WI, :], op=ALU.mult),
        drain=True)
    vop(lambda e: e.tensor_tensor(out=cst[:, C_T1, :], in0=cst[:, C_T1, :], in1=cst[:, C_T2, :], op=ALU.add),
        drain=True)
    vop(lambda e: e.tensor_tensor(out=cst[:, C_T1, :], in0=cst[:, C_T1, :], in1=cst[:, C_WI, :], op=ALU.mult),
        drain=True)
    vop(lambda e: e.tensor_scalar(out=cst[:, C_C, :], in0=cst[:, C_T1, :], scalar1=0.1, scalar2=None,
                                  op0=ALU.mult, op1=ALU.bypass), drain=True)
    vop(lambda e: e.tensor_scalar(out=cst[:, C_T3, :], in0=cst[:, C_B1, :], scalar1=0.1, scalar2=None,
                                  op0=ALU.mult, op1=ALU.bypass), drain=True)
    vop(lambda e: e.tensor_tensor(out=cst[:, C_TH1, :], in0=cst[:, C_TH, :], in1=cst[:, C_T3, :], op=ALU.add),
        drain=True)
    for i in range(7):
        ev_th1 = wrap_once(C_TH1, C_T1, bump_last=(i == 6))
    vop(lambda e: e.tensor_scalar(out=cst[:, C_T2, :], in0=cst[:, C_TH1, :],
                                  scalar1=PI / 2, scalar2=None, op0=ALU.add, op1=ALU.bypass),
        drain=True)
    ev_cosarg1 = wrap_once(C_T2, C_T1, lower=False, bump_last=True)
    ev_sth1 = sop(lambda e: e.activation(out=cst[:, C_STH1, :], in_=cst[:, C_TH1, :], func=AF.Sin),
                  waits=[ev_th1], bump=True)
    ev_cth1 = sop(lambda e: e.activation(out=cst[:, C_CTH1, :], in_=cst[:, C_T2, :], func=AF.Sin),
                  waits=[ev_cosarg1], bump=True)
    vop(lambda e: e.memset(kkv[:, :, :], 0.0))
    vop(lambda e: e.memset(cst[:, C_K, 0:1], LN_EPS))
    ev_consts = vop(lambda e: e.memset(cst[:, C_K, 1:2], PI / 2), bump=True, drain=True)

    def emit_ln(n_tiles, x_src, dst_view, tag):
        junk = pool_h.ap()[:, 3072:4096]
        for m in range(n_tiles):
            buf = m % 2
            ev_x = dma("sync", sx, xls[:, buf, :], x_src[m * 128:(m + 1) * 128, :],
                       waits=[st.get(("xls_reader", buf))])
            vop(lambda e, buf=buf, m=m: e.tensor_scalar(
                out=junk, in0=xls[:, buf, :], scalar1=1.0 / 1024, scalar2=None,
                op0=ALU.mult, op1=ALU.add, accum_out=bns[:, LM + m:LM + m + 1]),
                waits=[ev_x, ev_consts])
            ev_a = vop(lambda e, buf=buf, m=m: e.scalar_tensor_tensor(
                out=junk, in0=xls[:, buf, :], scalar=1.0 / 1024, in1=xls[:, buf, :],
                op0=ALU.mult, op1=ALU.mult, accum_out=bns[:, LV + m:LV + m + 1]), bump=True)
            st[("xls_reader", buf)] = ev_a
        n = n_tiles
        P.op("vector", lambda e: e.drain())
        vop(lambda e: e.tensor_tensor(out=bns[:, LJ:LJ + n], in0=bns[:, LM:LM + n],
                                      in1=bns[:, LM:LM + n], op=ALU.mult), drain=True)
        ev_var = vop(lambda e: e.tensor_tensor(out=bns[:, LV:LV + n], in0=bns[:, LV:LV + n],
                                               in1=bns[:, LJ:LJ + n], op=ALU.subtract),
                     bump=True, drain=True)
        ev_sq = sop(lambda e: e.activation(out=bns[:, LJ:LJ + n], in_=bns[:, LV:LV + n],
                                           func=AF.Sqrt, bias=cst[:, C_K, 0:1]),
                    waits=[ev_var], bump=True)
        vop(lambda e: e.reciprocal(out=bns[:, LR:LR + n], in_=bns[:, LJ:LJ + n]),
            waits=[ev_sq], drain=True)
        ev_fin = vop(lambda e: e.tensor_tensor(out=bns[:, LMR:LMR + n], in0=bns[:, LM:LM + n],
                                               in1=bns[:, LR:LR + n], op=ALU.mult),
                     bump=True, drain=True)
        ev_evac = []
        for m in range(n_tiles):
            buf = m % 2
            ev_x = dma("sync", sx, xls[:, buf, :], x_src[m * 128:(m + 1) * 128, :],
                       waits=[st.get(("xls_reader", buf))])
            xnrow = pool_h.ap()[:, buf * 2048:buf * 2048 + 512].bitcast(BF16)
            ev_xn = vop(lambda e, buf=buf, m=m, xr=xnrow: e.tensor_scalar(
                out=xr, in0=xls[:, buf, :], scalar1=bns[:, LR + m:LR + m + 1],
                scalar2=bns[:, LMR + m:LMR + m + 1], op0=ALU.mult, op1=ALU.subtract),
                waits=[ev_x, ev_fin], bump=True)
            st[("xls_reader", buf)] = ev_xn
            bA = 4 + 2 * (m % 2)
            for half in range(2):
                bk_i = bA + half
                tws = [bank_free[bk_i], ev_xn, ev_idb]
                ev = None
                for j in range(4):
                    dk = half * 4 + j
                    ev = top(lambda e, xr=xnrow, dk=dk, j=j, b=bk_i: e.transpose(
                        out=pb[b][:, 64 * j:64 * j + 64].bitcast(BF16),
                        in_=xr[:, dk * 128:(dk + 1) * 128], identity=idb),
                        waits=tws if j == 0 else (), bump=(j == 3))
                bank_data[bk_i] = ev
                ev_e = sop(lambda e, b=bk_i, half=half, m=m, dv=dst_view: e.copy(
                    out=dv[:, half * 4:half * 4 + 4, m * 128:(m + 1) * 128],
                    in_=pb[b][:, 0:256].bitcast(BF16).rearrange("p (a b) -> p a b", a=4)),
                    waits=[bank_data[bk_i]], bump=True)
                bank_free[bk_i] = ev_e
                ev_evac.append(ev_e)
        return ev_evac

    ev_xnT_own = emit_ln(NT128, xo, xnT, "own")
    if stage == "ln":
        inc = sy.bump(16)
        P.op("sync", lambda e: e.dma_start(out=dbg["xnT"], in_=xnT_h.ap()),
             waits=[ev_xnT_own[-1]], incs=[inc])
        for i, ccol in enumerate([C_THW, C_STH, C_CTH, C_C, C_STH1, C_CTH1]):
            inc = sy.bump(16)
            P.op("sync", lambda e, i=i, ccol=ccol: e.dma_start(
                out=dbg["cst"][:, 32 * i:32 * (i + 1)], in_=cst[:, ccol, :]),
                waits=[ev_cth1, ev_sth1, ev_cth], incs=[inc])
        P.op("gpsimd", lambda e: e.wait_ge(sy.h, sy.n))
        _finish(nc, P)
        stack.close()
        return nc

    ev_xnT_oth = emit_ln(NT128, xt, xnO, "oth")

    ev_wk = dma("gpsimd", sw, wsm[:, 0, :, :, :],
                wk_d.rearrange("(a p) (b f) -> p a b f", p=128, b=8))
    ev_wv = dma("gpsimd", sw, wsm[:, 1, :, :, :],
                wv_d.rearrange("(a p) (b f) -> p a b f", p=128, b=8))

    slices = [("own", s) for s in range(4)] + [("oth", s) for s in range(4)]

    def xn_src(kind, s, dk):
        v = xnT if kind == "own" else xnO
        return v[:, dk, s * 512:(s + 1) * 512]

    def xn_ready(kind, s):
        evs = ev_xnT_own if kind == "own" else ev_xnT_oth
        return evs[(s * 4 + 3) * 2 + 1]

    ev_wq = ev_wo = None
    for hd in range(NHD):
        ev_last_kmm = ev_last_vmm = None
        for si, (kind, s) in enumerate(slices):
            step = hd * NSL + si
            kb = (step % 2) * 2
            vb = kb + 1
            ev = None
            for dk in range(DK):
                w = [bank_free[kb], xn_ready(kind, s), ev_wk] if dk == 0 else ()
                ev = top(lambda e, dk=dk, hd=hd, kind=kind, s=s, b=kb: e.matmul(
                    pb[b][:, :], wsm[:, 0, dk, hd, :], xn_src(kind, s, dk),
                    start=(dk == 0), stop=(dk == DK - 1)), waits=w, bump=(dk == DK - 1))
            bank_data[kb] = ev
            ev_last_kmm = ev
            for dk in range(DK):
                w = [bank_free[vb], ev_wv] if dk == 0 else ()
                ev = top(lambda e, dk=dk, hd=hd, kind=kind, s=s, b=vb: e.matmul(
                    pb[b][:, :], wsm[:, 1, dk, hd, :], xn_src(kind, s, dk),
                    start=(dk == 0), stop=(dk == DK - 1)), waits=w, bump=(dk == DK - 1))
            bank_data[vb] = ev
            ev_last_vmm = ev
            par = step % 2
            ks_s = pool[:, 0 + par, :]
            mn_s = pool[:, 2 + par, :]
            e_s = pool[:, 4 + par, :]
            K_s = pool[:, 6 + par, :]
            ev_ks = sop(lambda e, b=kb, hd=hd, o=ks_s: e.activation(
                out=o, in_=pb[b][:, :], func=AF.Identity, bias=bia[:, B_K, hd:hd + 1]),
                waits=[bank_data[kb]], bump=True)
            bank_free[kb] = ev_ks
            ev_mn = vop(lambda e, i=ks_s, o=mn_s: e.tensor_scalar(
                out=o, in0=i, scalar1=0.0, scalar2=None, op0=ALU.min, op1=ALU.bypass),
                waits=[ev_ks], bump=True)
            ev_e = sop(lambda e, i=mn_s, o=e_s: e.activation(out=o, in_=i, func=AF.Exp),
                       waits=[ev_mn], bump=True)
            ev_K = vop(lambda e, i=ks_s, es=e_s, o=K_s, hd=hd, si=si: e.scalar_tensor_tensor(
                out=o, in0=i, scalar=0.0, in1=es, op0=ALU.max, op1=ALU.add,
                accum_out=kac[:, 0, hd, si:si + 1]), waits=[ev_e], bump=True)
            ev_kv = vop(lambda e, Ks=K_s, b=vb, o=mn_s, hd=hd, si=si: e.scalar_tensor_tensor(
                out=o, in0=Ks, scalar=1.0, in1=pb[b][:, :], op0=ALU.mult, op1=ALU.mult,
                accum_out=kac[:, 1, hd, si:si + 1]),
                waits=[ev_K, bank_data[vb]], bump=True)
            bank_free[vb] = ev_kv
        ev_wq = dma("gpsimd", sw, wsm[:, 0, :, hd, :],
                    wq_d.rearrange("(a p) (b f) -> p a b f", p=128, b=8)[:, :, hd, :],
                    waits=[ev_last_kmm])
        ev_wo = dma("gpsimd", sw, wsm[:, 1, :, hd, :],
                    wo_d.rearrange("(a p) (b f) -> p a b f", p=128, b=8)[:, :, hd, :],
                    waits=[ev_last_vmm])
    P.op("vector", lambda e: e.drain())
    vop(lambda e: e.reduce_sum(out=kkv[:, 0, :], in_=kac[:, 0, :, :], axis=AX), drain=True)
    vop(lambda e: e.reduce_sum(out=kkv[:, 1, :], in_=kac[:, 1, :, :], axis=AX), drain=True)
    vop(lambda e: e.tensor_tensor(out=kac[:, 0, 0, :], in0=bia[:, B_V, :], in1=kkv[:, 0, :],
                                  op=ALU.mult), drain=True)
    ev_ks_done = vop(lambda e: e.tensor_tensor(out=kkv[:, 1, :], in0=kkv[:, 1, :],
                                               in1=kac[:, 0, 0, :], op=ALU.add),
                     bump=True, drain=True)

    if stage == "a1":
        inc = sy.bump(16)
        P.op("sync", lambda e: e.dma_start(out=dbg["ks"], in_=kkv[:, 0, :]), waits=[ev_ks_done], incs=[inc])
        inc = sy.bump(16)
        P.op("sync", lambda e: e.dma_start(out=dbg["kv"], in_=kkv[:, 1, :]), incs=[inc])
        P.op("gpsimd", lambda e: e.wait_ge(sy.h, sy.n))
        _finish(nc, P)
        stack.close()
        return nc

    ev_w1_slot = [None, None]
    ev_w2_slot = [None, None]
    free_w1_slot = [None, None]
    free_w2_slot = [None, None]
    do_ffn = stage in ("full", "fulldbg")

    for tt in range(NTT):
        off = tt * TT
        ev_xr = dma("sync", sx, xrs[:, :, :],
                    xo[off:off + TT, :].rearrange("(a p) d -> p a d", p=128),
                    waits=[st.get(("yadd_done", tt - 1))])

        def load_w1(g, extra=()):
            slot = g % 2
            st[("w1ld", tt, g)] = dma(
                "gpsimd", sw, w1r[:, slot, :, :],
                w1_d[:, g * 512:(g + 1) * 512].rearrange("(a p) f -> p a f", p=128),
                waits=[free_w1_slot[slot]] + list(extra))

        def load_w2(half, blk):
            slot = blk % 2
            st[("w2ld", tt, half, blk)] = dma(
                "gpsimd", sw, w2r[:, slot, :, :],
                w2_d[blk * 512:(blk + 1) * 512, half * 512:(half + 1) * 512]
                .rearrange("(a p) f -> p a f", p=128), waits=[free_w2_slot[slot]])

        if do_ffn:
            load_w1(0, extra=([ev_wq, ev_wo] if tt == 0 else []))
            load_w1(1)

        for hd in range(NHD):
            qb = hd % 2
            ev = None
            for dk in range(DK):
                w = [bank_free[qb], ev_wq, ev_ks_done] if dk == 0 else ()
                ev = top(lambda e, dk=dk, hd=hd, b=qb, off=off: e.matmul(
                    pb[b][:, :], wsm[:, 0, dk, hd, :], xnT[:, dk, off:off + 512],
                    start=(dk == 0), stop=(dk == DK - 1)), waits=w, bump=(dk == DK - 1))
            bank_data[qb] = ev
            par = hd % 2
            q_s = pool[:, 0 + par, :]
            mn_s = pool[:, 2 + par, :]
            e_s = pool[:, 4 + par, :]
            Q_s = pool[:, 6 + par, :]
            ev_qs = sop(lambda e, b=qb, hd=hd, o=q_s: e.activation(
                out=o, in_=pb[b][:, :], func=AF.Identity, bias=bia[:, B_Q, hd:hd + 1]),
                waits=[bank_data[qb]], bump=True)
            bank_free[qb] = ev_qs
            ev_mn = vop(lambda e, i=q_s, o=mn_s: e.tensor_scalar(
                out=o, in0=i, scalar1=0.0, scalar2=None, op0=ALU.min, op1=ALU.bypass),
                waits=[ev_qs], bump=True)
            ev_e = sop(lambda e, i=mn_s, o=e_s: e.activation(out=o, in_=i, func=AF.Exp),
                       waits=[ev_mn], bump=True)
            ev_Q = vop(lambda e, i=q_s, es=e_s, o=Q_s: e.scalar_tensor_tensor(
                out=o, in0=i, scalar=0.0, in1=es, op0=ALU.max, op1=ALU.add),
                waits=[ev_e], bump=True)
            d_s = pool[:, 2 + par, :]
            ev_d = vop(lambda e, i=Q_s, o=d_s, hd=hd: e.tensor_scalar(
                out=o, in0=i, scalar1=kkv[:, 0, hd:hd + 1], scalar2=ATTN_EPS,
                op0=ALU.mult, op1=ALU.add), bump=True)
            lnd_s = pool[:, 0 + par, :]
            ev_ln = sop(lambda e, i=d_s, o=lnd_s: e.activation(out=o, in_=i, func=AF.Ln),
                        waits=[ev_d], bump=True)
            r_s = pool[:, 4 + par, :]
            ev_r = sop(lambda e, i=lnd_s, o=r_s: e.activation(out=o, in_=i, func=AF.Exp, scale=-1.0),
                       waits=[ev_ln], bump=True)
            n_s = pool[:, 2 + par, :]
            vop(lambda e, i=Q_s, o=n_s, hd=hd: e.tensor_scalar(
                out=o, in0=i, scalar1=kkv[:, 1, hd:hd + 1], scalar2=None,
                op0=ALU.mult, op1=ALU.bypass), waits=[ev_r])
            ev_at = vop(lambda e, i=n_s, rr=r_s, hd=hd: e.tensor_tensor(
                out=attn[:, hd, :], in0=i, in1=rr, op=ALU.mult),
                waits=[st.get(("a2b_done", tt - 1))], bump=True)
            st[("attn", tt, hd)] = ev_at

        ev = None
        for dc in range(DK):
            ob = 2 + dc % 2
            for hk in range(NHD):
                w = [bank_free[ob], ev_wo, st[("attn", tt, NHD - 1)]] if hk == 0 else ()
                ev = top(lambda e, hk=hk, dc=dc, b=ob: e.matmul(
                    pb[b][:, :], wsm[:, 1, hk, dc, :], attn[:, hk, :],
                    start=(hk == 0), stop=(hk == NHD - 1)), waits=w, bump=(hk == NHD - 1))
            bank_data[ob] = ev
            ev_g = sop(lambda e, b=ob, dc=dc: e.activation(
                out=gfm[:, dc, :], in_=pb[b][:, :], func=AF.Gelu_apprx_tanh,
                bias=bia[:, B_O, dc:dc + 1]),
                waits=[bank_data[ob], st.get(("gfree", tt - 1, dc))], bump=True)
            bank_free[ob] = ev_g
            st[("gelu", tt, dc)] = ev_g
        st[("a2b_done", tt)] = ev

        if do_ffn:
            for fo in range(DFF):
                hb = fo % 2
                g = fo // 4
                fw = fo % 4
                slot = g % 2
                ev = None
                for dk in range(DK):
                    w = [bank_free[hb], st[("w1ld", tt, g)]] if dk == 0 else ()
                    ev = top(lambda e, dk=dk, fw=fw, slot=slot, b=hb, off=off: e.matmul(
                        pb[b][:, :], w1r[:, slot, dk, fw * 128:(fw + 1) * 128],
                        xnT[:, dk, off:off + 512],
                        start=(dk == 0), stop=(dk == DK - 1)), waits=w, bump=(dk == DK - 1))
                bank_data[hb] = ev
                if fo % 4 == 3:
                    free_w1_slot[slot] = ev
                    if g + 2 <= 7:
                        load_w1(g + 2)
                    elif g == 6:
                        load_w2(0, 0)
                    elif g == 7:
                        load_w2(0, 1)
                par = fo % 2
                sd_s = pool[:, 0 + par, :]
                cd_s = pool[:, 2 + par, :]
                t1_s = pool[:, 4 + par, :]
                su_s = pool[:, 6 + par, :]
                qp_s = pool[:, 4 + par, :]
                ev_sd = sop(lambda e, b=hb, o=sd_s: e.activation(
                    out=o, in_=pb[b][:, :], func=AF.Sin, scale=0.1),
                    waits=[bank_data[hb], st[("attn", tt, NHD - 1)]], bump=True)
                ev_cd = sop(lambda e, b=hb, o=cd_s: e.activation(
                    out=o, in_=pb[b][:, :], func=AF.Sin, scale=0.1, bias=cst[:, C_K, 1:2]), bump=True)
                vop(lambda e, i=sd_s, o=t1_s, fo=fo: e.tensor_scalar(
                    out=o, in0=i, scalar1=cst[:, C_CTH1, fo:fo + 1], scalar2=None,
                    op0=ALU.mult, op1=ALU.bypass), waits=[ev_sd, ev_cth1])
                vop(lambda e, i=cd_s, t1=t1_s, o=su_s, fo=fo: e.scalar_tensor_tensor(
                    out=o, in0=i, scalar=cst[:, C_STH1, fo:fo + 1], in1=t1,
                    op0=ALU.mult, op1=ALU.add), waits=[ev_cd, ev_sth1])
                ev_qp = vop(lambda e, i=su_s, b=hb, o=qp_s, fo=fo: e.scalar_tensor_tensor(
                    out=o, in0=i, scalar=cst[:, C_C, fo:fo + 1], in1=pb[b][:, :],
                    op0=ALU.mult, op1=ALU.add), bump=True)
                bank_free[hb] = ev_qp
                vop(lambda e, i=qp_s, fo=fo: e.tensor_scalar(
                    out=actv[:, fo, :], in0=i, scalar1=cst[:, C_B1, fo:fo + 1], scalar2=0.0,
                    op0=ALU.add, op1=ALU.max), bump=True)
                st[("act", tt, fo)] = sV.evt()

            for half in range(2):
                ev = None
                for fi in range(DFF):
                    blk = fi // 4
                    fiw = fi % 4
                    slot = blk % 2
                    for do in range(4):
                        b = 4 + do
                        w = []
                        if fi == 0 and do == 0:
                            w = [bank_free[4], bank_free[5], bank_free[6], bank_free[7],
                                 st[("act", tt, DFF - 1)], st[("w2ld", tt, half, blk)]]
                        elif fiw == 0 and do == 0:
                            w = [st[("w2ld", tt, half, blk)]]
                        ev = top(lambda e, fiw=fiw, slot=slot, do=do, fi=fi, b=b: e.matmul(
                            pb[b][:, :], w2r[:, slot, fiw, do * 128:(do + 1) * 128], actv[:, fi, :],
                            start=(fi == 0), stop=(fi == DFF - 1)),
                            waits=w, bump=(fiw == 3 and do == 3))
                    if fiw == 3:
                        free_w2_slot[slot] = ev
                        nhalf, nblk = half, blk + 2
                        if nblk > 7:
                            nhalf, nblk = half + 1, nblk - 8
                        if nhalf <= 1:
                            load_w2(nhalf, nblk)
                for do in range(4):
                    b = 4 + do
                    dc = half * 4 + do
                    ev_s = vop(lambda e, b=b, dc=dc: e.scalar_tensor_tensor(
                        out=gfm[:, dc, :], in0=pb[b][:, :], scalar=bia[:, B_2, dc:dc + 1],
                        in1=gfm[:, dc, :], op0=ALU.add, op1=ALU.add),
                        waits=[ev, st[("gelu", tt, dc)]], bump=True)
                    bank_free[b] = ev_s
                    st[("sfm", tt, dc)] = ev_s
        else:
            for dc in range(8):
                st[("sfm", tt, dc)] = st[("gelu", tt, dc)]

        for dc in range(DK):
            yb = 2 + dc % 2
            tws = [bank_free[yb], st[("sfm", tt, dc)], ev_idf]
            ev = None
            for sub in range(4):
                ev = top(lambda e, dc=dc, sub=sub, b=yb: e.transpose(
                    out=pb[b][:, sub * 128:(sub + 1) * 128],
                    in_=gfm[:, dc, sub * 128:(sub + 1) * 128], identity=idf),
                    waits=tws if sub == 0 else (), bump=(sub == 3))
            bank_data[yb] = ev
            st[("gfree", tt, dc)] = ev
            ev_add = vop(lambda e, dc=dc, b=yb: e.tensor_tensor(
                out=ysb[:, :, dc * 128:(dc + 1) * 128],
                in0=pb[b][:, :].rearrange("p (a b) -> p a b", a=4),
                in1=xrs[:, :, dc * 128:(dc + 1) * 128], op=ALU.add),
                waits=[bank_data[yb], ev_xr, st.get(("ydma", tt - 1))], bump=True)
            bank_free[yb] = ev_add
        st[("yadd_done", tt)] = ev_add
        inc = sy.bump(16)
        P.op("sync", lambda e, off=off: e.dma_start(
            out=y_d[off:off + TT, :].rearrange("(a p) d -> p a d", p=128), in_=ysb[:, :, :]),
            waits=[st[("yadd_done", tt)]], incs=[inc])
        st[("ydma", tt)] = sy.evt()

    if stage == "fulldbg":
        inc = sy.bump(16)
        P.op("sync", lambda e: e.dma_start(out=dbg["act"], in_=act_h.ap()),
             waits=[st[("act", NTT - 1, DFF - 1)]], incs=[inc])
    P.op("gpsimd", lambda e: e.wait_ge(sy.h, sy.n))
    _finish(nc, P)
    stack.close()
    return nc


def _finish(nc, P):
    with nc.Block() as block:
        @block.sync
        def _(e):
            for f in P.q["sync"]:
                f(e)

        @block.gpsimd
        def _(e):
            for f in P.q["gpsimd"]:
                f(e)

        @block.vector
        def _(e):
            for f in P.q["vector"]:
                f(e)

        @block.scalar
        def _(e):
            for f in P.q["scalar"]:
                f(e)

        @block.tensor
        def _(e):
            for f in P.q["tensor"]:
                f(e)


def _pm(v):
    v = np.asarray(v, np.float32).reshape(-1)
    return np.ascontiguousarray(v.reshape(-1, 128).T)


def _in_maps(inputs):
    x = np.ascontiguousarray(np.asarray(inputs["L0_attn_x"], dtype=np.float32))
    common = {
        "wq": np.ascontiguousarray(np.asarray(inputs["L0_attn_wq"], np.float32).reshape(1024, 1024)),
        "wk": np.ascontiguousarray(np.asarray(inputs["L0_attn_wk"], np.float32).reshape(1024, 1024)),
        "wv": np.ascontiguousarray(np.asarray(inputs["L0_attn_wv"], np.float32).reshape(1024, 1024)),
        "wo": np.ascontiguousarray(np.asarray(inputs["L0_attn_wo"], np.float32).reshape(1024, 1024)),
        "w1": np.ascontiguousarray(np.asarray(inputs["L0_ffn_W1"], np.float32)),
        "w2": np.ascontiguousarray(np.asarray(inputs["L0_ffn_W2"], np.float32)),
        "bq": _pm(inputs["L0_attn_bq"]),
        "bk": _pm(inputs["L0_attn_bk"]),
        "bv": _pm(inputs["L0_attn_bv"]),
        "bo": _pm(inputs["L0_attn_bo"]),
        "b2": _pm(inputs["L0_ffn_b2"]),
        "b1": _pm(inputs["L0_ffn_b1"]),
        "theta": _pm(inputs["L0_ffn_quantum_phases"]),
        "wreal": _pm(inputs["L0_ffn_quantum_weights_real"]),
        "wimag": _pm(inputs["L0_ffn_quantum_weights_imag"]),
        "idb": np.eye(128, dtype=np.float32).astype(ml_dtypes.bfloat16),
        "idf": np.eye(128, dtype=np.float32),
    }
    maps = []
    for c in range(8):
        b, h = c // 2, c % 2
        m = dict(common)
        m["x_own"] = np.ascontiguousarray(x[b, h * TOK:(h + 1) * TOK, :])
        m["x_oth"] = np.ascontiguousarray(x[b, (1 - h) * TOK:(2 - h) * TOK, :])
        maps.append(m)
    return maps


def kernel(**inputs):
    nc = build_nc("full")
    maps = _in_maps(inputs)
    res = run_bass_kernel_spmd(nc, maps, core_ids=list(range(8)))
    out = np.empty((4, 4096, 1024), dtype=np.float32)
    for c in range(8):
        b, h = c // 2, c % 2
        out[b, h * TOK:(h + 1) * TOK, :] = res.results[c]["y"]
    return out
